# version 12 (final)
# speedup vs baseline: 1.1394x; 1.0025x over previous
import contextlib

import numpy as np

import concourse.bass as bass
import concourse.bacc as bacc
import concourse.mybir as mybir
from concourse.bass_utils import run_bass_kernel_spmd

F32 = mybir.dt.float32
F32R = mybir.dt.float32r
BF16 = mybir.dt.bfloat16
ACT = mybir.ActivationFunctionType
ALU = mybir.AluOpType
AX = mybir.AxisListType.X

D = 1024
DI = 2048
DS = 16
DH = 4096
T = 16384
NCORES = 8
TOWN = 2048
TW = 2176
NT = 17
YC0, YC1 = 3, 2053
NY = YC1 - YC0
NSC = 2049
NS1 = 2048
HW2 = 1025
GW = 256
EPS_LN1 = 1e-5
EPS_IN = 1e-6
B52 = float(2.0**52)
A23 = float(2.0**23 + 2.0**22)
ARENA_BYTES = 157184
SZ = {F32: 4, F32R: 4, BF16: 2}


def chunks(lo, hi, step):
    out = []
    a = lo
    while a < hi:
        out.append((a, min(hi, a + step)))
        a = out[-1][1]
    return out


CHY = chunks(YC0, YC1, 410)


class Sem:
    __slots__ = ("h", "n")

    def __init__(self, h):
        self.h = h
        self.n = 0


class PT:
    __slots__ = ("t", "mark")

    def __init__(self, t):
        self.t = t
        self.mark = 0


class B:
    def __init__(self):
        self.nc = bacc.Bacc("TRN2", target_bir_lowering=False)
        self.stack = contextlib.ExitStack()
        self.steps = {e: [] for e in ("sync", "scalar", "vector", "tensor", "gpsimd")}
        self._id = 0
        self.arena = self.stack.enter_context(
            self.nc.sbuf_tensor("arena", [128, ARENA_BYTES // 4], F32))
        self.abase = self.nc.lookup_mloc(self.arena).addr

    def sem(self, name=None):
        self._id += 1
        return Sem(self.stack.enter_context(
            self.nc.semaphore(name or f"sm{self._id}")))

    def sbp(self, name, shape, dtype=F32):
        return self.stack.enter_context(self.nc.sbuf_tensor(name, shape, dtype))

    def at(self, name, shape, dtype, off):
        assert off + shape[1] * SZ[dtype] <= ARENA_BYTES, (name, off)
        return self.nc.alloc_sbuf_tensor_at(name, shape, dtype,
                                            offset=self.abase + off)

    def pt(self, name, shape):
        return PT(self.stack.enter_context(self.nc.psum_tensor(name, shape, F32)))

    def on(self, eng, fn):
        self.steps[eng].append(fn)

    def wait(self, eng, sem, val=None):
        v = sem.n if val is None else val
        if v > 0:
            self.on(eng, lambda e, h=sem.h, v=v: e.wait_ge(h, v))

    def run(self, eng, make, sem=None, k=1):
        if sem is None:
            self.on(eng, lambda e, m=make: m(e))
        else:
            self.on(eng, lambda e, m=make, h=sem.h, k=k: m(e).then_inc(h, k))
            sem.n += k

    def finalize(self):
        steps = self.steps
        with self.nc.Block() as block:
            @block.sync
            def _(e):
                for fn in steps["sync"]:
                    fn(e)

            @block.scalar
            def _(e):
                for fn in steps["scalar"]:
                    fn(e)

            @block.vector
            def _(e):
                for fn in steps["vector"]:
                    fn(e)

            @block.tensor
            def _(e):
                for fn in steps["tensor"]:
                    fn(e)

            @block.gpsimd
            def _(e):
                for fn in steps["gpsimd"]:
                    fn(e)
        self.nc.finalize()
        return self.nc


def build_core_program(summary_only):
    b = B()
    nc = b.nc

    dx = nc.dram_tensor("x", [TW, D], F32, kind="ExternalInput").ap()
    dWin = nc.dram_tensor("W_in", [D, 2 * DI], F32, kind="ExternalInput").ap()
    dWdt = nc.dram_tensor("W_dt", [DI, DI], F32, kind="ExternalInput").ap()
    dWxp = nc.dram_tensor("W_xp", [DI, 2 * DS], F32, kind="ExternalInput").ap()
    dbin = nc.dram_tensor("bin_c", [128, 32], F32, kind="ExternalInput").ap()
    dbdt = nc.dram_tensor("bdt_c", [128, 16], F32, kind="ExternalInput").ap()
    dcb = nc.dram_tensor("cb_c", [128, 16], F32, kind="ExternalInput").ap()
    dcw = nc.dram_tensor("cw_c", [128, 64], F32, kind="ExternalInput").ap()
    dbxp = nc.dram_tensor("bxp_c", [32, 1], F32, kind="ExternalInput").ap()
    dalog = nc.dram_tensor("alog_c", [128, 256], F32, kind="ExternalInput").ap()
    dmisc = nc.dram_tensor("misc_c", [128, 16], F32, kind="ExternalInput").ap()
    spk = "ExternalOutput" if summary_only else "ExternalInput"
    dxcv = nc.dram_tensor("xconv_sp", [DI, NY], F32, kind=spk).ap()
    ddel = nc.dram_tensor("delta_sp", [DI, NY], F32, kind=spk).ap()
    dzsp = nc.dram_tensor("z_sp", [DI, NY], F32, kind=spk).ap()
    dbc = nc.dram_tensor("bc_sp", [32, NY], F32, kind=spk).ap()
    if summary_only:
        dh0 = nc.dram_tensor("h0sum", [128, 256], F32, kind="ExternalOutput").ap()
        dds = nc.dram_tensor("dsum", [128, 16], F32, kind="ExternalOutput").ap()
    else:
        dWout = nc.dram_tensor("W_out", [DI, D], F32, kind="ExternalInput").ap()
        dw1 = nc.dram_tensor("w1", [DH, D], F32, kind="ExternalInput").ap()
        dw2 = nc.dram_tensor("w2", [D, DH], F32, kind="ExternalInput").ap()
        ddsk = nc.dram_tensor("dskip_c", [128, 16], F32, kind="ExternalInput").ap()
        dabb = nc.dram_tensor("abb_r", [4, D], F32, kind="ExternalInput").ap()
        dfcb = nc.dram_tensor("fcb_c", [128, 32], F32, kind="ExternalInput").ap()
        dfcw = nc.dram_tensor("fcw_c", [128, 96], F32, kind="ExternalInput").ap()
        dh0all = nc.dram_tensor("h0all", [128, NCORES * 256], F32,
                                kind="ExternalInput").ap()
        ddsall = nc.dram_tensor("dsall", [128, NCORES * 16], F32,
                                kind="ExternalInput").ap()
        dout = nc.dram_tensor("out", [TOWN, D], F32, kind="ExternalOutput").ap()
        dysp = nc.dram_tensor("y_sp", [DI, NY], F32, kind="Internal").ap()
        dgsp = nc.dram_tensor("g_sp", [DI, NY], BF16, kind="Internal").ap()
        dxp = nc.dram_tensor("xp_sp", [TW, D], F32, kind="Internal").ap()
        dhs = nc.dram_tensor("hs_sp", [TOWN, DH], F32, kind="Internal").ap()

    cols = b.sbp("cols", [128, 1024])
    colw = b.sbp("colw", [32, 8])
    BC = b.sbp("BC", [32, NY])
    BCr = BC.bitcast(F32R)
    ident = b.sbp("ident", [128, 128])
    identb = b.sbp("identb", [128, 128], BF16)
    identf = b.sbp("identf", [128, 128], F32R)
    fdiag = b.sbp("fdiag", [128, 384], F32R)
    ones1 = b.sbp("ones1", [1, 128])
    ones32 = b.sbp("ones32", [32, 128])
    selw = b.sbp("selw", [32, 128], F32R)
    selwf = selw.bitcast(F32)
    A_sb = b.sbp("A_sb", [128, 256])
    sumh = b.sbp("sumh", [128, 256])
    sumd = b.sbp("sumd", [128, 16])
    hcar = b.sbp("hcar", [128, 256])
    work = b.sbp("work", [128, 4300])

    x_sb = b.at("x_sb", [128, NT * D], F32, 0)
    ln1T = b.at("ln1T", [128, 8 * TW], BF16, 69632)
    xcb = b.at("xcb", [128, 16 * NY], BF16, 0)
    p3x = b.at("p3x", [128, TW], F32R, 104448)
    p3xf = p3x.bitcast(F32)
    p3c = b.at("p3c", [128, NY], F32, 113152)
    pdiag = b.at("pdiag", [128, 512], F32R, 121376)
    wst = b.at("wst", [128, 8 * DI], BF16, 123424)
    xpst = b.at("xpst", [128, 16 * 32], BF16, 86080)
    wdtst = b.at("wdtst", [128, 16 * 128], BF16, 148000)
    wdtst2 = b.at("wdtst2", [128, 16 * 128], BF16, 152128)
    wpanA = b.at("wpanA", [128, 4096], F32, 90112)
    wpanB = b.at("wpanB", [128, 4096], F32, 106496)
    dj_w = b.at("dj_w", [128, NY], F32, 69632)
    dj2_w = b.at("dj2_w", [128, NY], F32, 77856)
    brepf = b.at("brepf", [128, 16 * HW2], F32, 0)
    djh = b.at("djh", [128, HW2], F32, 131200)
    ujh = b.at("ujh", [128, HW2], F32, 135328)
    sc1 = b.at("sc1", [128, HW2], F32, 139456)
    sc2 = b.at("sc2", [128, HW2], F32, 143584)
    sc3 = b.at("sc3", [128, HW2], F32, 147712)
    if not summary_only:
        crepf = b.at("crepf", [128, 16 * HW2], F32, 65600)
        sc4 = b.at("sc4", [128, HW2], F32R, 151840)
        sc4f = sc4.bitcast(F32)
        yg = b.at("yg", [128, NY], F32, 0)
        ug = b.at("ug", [128, NY], F32, 8224)
        dg = b.at("dg", [128, NY], F32, 16448)
        ygb = b.at("ygb", [128, NY], BF16, 24672)
        yg2 = b.at("yg2", [128, NY], F32, 32896)
        ug2 = b.at("ug2", [128, NY], F32, 41120)
        dg2 = b.at("dg2", [128, NY], F32, 49344)
        ygb2 = b.at("ygb2", [128, NY], BF16, 57568)
        x2 = b.at("x2", [128, NT * D], F32, 0)
        woutst = b.at("woutst", [128, 16 * 1024], BF16, 69632)
        gst = b.at("gst", [128, 16 * 128], BF16, 135168)
        gstB = b.at("gstB", [128, 16 * 128], BF16, 104448)
        abbr = b.at("abbr", [128, 3 * 1024], F32, 143360)
        xq = b.at("xq", [128, NT * D], BF16, 69632)
        xqT = b.at("xqT", [128, 8 * TW], BF16, 104448)
        w1qT = b.at("w1qT", [128, 8 * DH], BF16, 0)
        djq1 = b.at("djq1", [128, 1024], F32, 139264)
        h1g = b.at("h1g", [128, NT * GW], F32, 65536)
        h1T = b.at("h1T", [128, 2 * 2056], F32, 82944)
        h1Tr = h1T.bitcast(F32R)
        hsj = b.at("hsj", [128, 2048], F32, 139264)
        hst = b.at("hst", [128, 2048], F32, 147456)
        w2qT = b.at("w2qT", [128, 32 * 1024], BF16, 0)
        djq2 = b.at("djq2", [128, 2048], F32, 69632)
        dj2q2 = b.at("dj2q2", [128, 2048], F32, 77824)
        w2s1 = b.at("w2s1", [128, 2048], F32, 86016)
        w2s2 = b.at("w2s2", [128, 2048], F32, 94208)
        hs2 = b.at("hs2", [128, 4096], F32, 65536)
        hs2b = b.at("hs2b", [128, 4096], F32, 98304)
        xq2 = b.at("xq2", [128, 4096], BF16, 81920)
        xq2T = b.at("xq2T", [128, 32 * 128], BF16, 90112)
        xq2b = b.at("xq2b", [128, 4096], BF16, 114688)
        xq2Tb = b.at("xq2Tb", [128, 32 * 128], BF16, 122880)

    ps_a = b.pt("ps_a", [128, 512])
    ps_b = b.pt("ps_b", [128, 512])
    ps_t = b.pt("ps_t", [128, 512])
    ps_tb = ps_t.t.bitcast(BF16)
    ps_w = b.pt("ps_w", [128, 5 * 512])

    s_ld = b.sem("ld")
    s_w = b.sem("w")
    s_wA = b.sem("wA")
    s_wB = b.sem("wB")
    s_spA = b.sem("spA")
    s_spB = b.sem("spB")
    s_pv = b.sem("pv")
    s_ev = b.sem("ev")
    s_v = b.sem("v")
    s_s = b.sem("s")
    s_g = b.sem("g")
    s_sp = b.sem("sp")
    s_rs = b.sem("rs")
    s_out = b.sem("out")

    W = b.wait
    R = b.run

    def mm_group(pt, makes):
        W("tensor", s_ev, pt.mark)
        for i, mk in enumerate(makes):
            R("tensor", mk, s_pv if i == len(makes) - 1 else None, 1)

    def evac(pt, make, eng="scalar"):
        W(eng, s_pv)
        R(eng, make, s_ev, 1)
        pt.mark = s_ev.n

    def ln_stats(src_ap_fn, c0, p0=0, p1=128):
        W("vector", s_v)
        R("vector", lambda e, p0=p0, p1=p1: e.bn_stats(
            out=cols[p0:p1, 300:306], in_=src_ap_fn(0, 512)), s_v, 1)
        R("vector", lambda e, p0=p0, p1=p1: e.bn_stats(
            out=cols[p0:p1, 306:312], in_=src_ap_fn(512, 1024)), s_v, 1)
        W("vector", s_v)
        R("vector", lambda e, c0=c0, p0=p0, p1=p1: e.bn_aggr(
            out=cols[p0:p1, c0:c0 + 2], in_=cols[p0:p1, 300:312]), s_v, 1)

    def emit_round(dst_fn, src_fn, scale_col, scr1_fn, scr2_fn):
        W("vector", s_v)
        R("vector", lambda e: e.tensor_scalar(
            out=scr1_fn(), in0=src_fn(),
            scalar1=scale_col, scalar2=A23, op0=ALU.mult, op1=ALU.add), s_v, 1)
        W("vector", s_v)
        R("vector", lambda e: e.tensor_scalar(
            out=scr1_fn(), in0=scr1_fn(), scalar1=-A23, scalar2=None,
            op0=ALU.add), s_v, 1)
        W("vector", s_v)
        R("vector", lambda e: e.tensor_scalar(
            out=scr2_fn(), in0=scr1_fn(),
            scalar1=B52, scalar2=-B52, op0=ALU.add, op1=ALU.add), s_v, 1)
        W("vector", s_v)
        R("vector", lambda e: e.tensor_scalar(
            out=scr1_fn(), in0=scr1_fn(),
            scalar1=cols[:, 291:292], scalar2=None, op0=ALU.mult), s_v, 1)
        W("vector", s_v)
        R("vector", lambda e: e.scalar_tensor_tensor(
            out=dst_fn(), in0=scr2_fn(), scalar=cols[:, 290:291], in1=scr1_fn(),
            op0=ALU.mult, op1=ALU.add), s_v, 1)

    if summary_only:
        for t in range(NT):
            R("sync", lambda e, t=t: e.dma_start(
                out=x_sb[:, t * D:(t + 1) * D],
                in_=dx[t * 128:(t + 1) * 128, :]), s_ld, 16)
        R("sync", lambda e: e.dma_start(out=cols[:, 0:32], in_=dbin), s_ld, 16)
        R("sync", lambda e: e.dma_start(out=cols[:, 32:48], in_=dbdt), s_ld, 16)
        R("sync", lambda e: e.dma_start(out=cols[:, 48:64], in_=dcb), s_ld, 16)
        R("sync", lambda e: e.dma_start(out=cols[:, 64:128], in_=dcw), s_ld, 16)
        R("sync", lambda e: e.dma_start(out=colw[:, 0:1], in_=dbxp), s_ld, 16)
    R("sync", lambda e: e.dma_start(out=cols[:, 128:144], in_=dmisc), s_ld, 16)
    R("sync", lambda e: e.dma_start(out=A_sb[:], in_=dalog), s_ld, 16)
    if not summary_only:
        R("sync", lambda e: e.dma_start(out=cols[:, 144:160], in_=ddsk), s_ld, 16)
        R("sync", lambda e: e.dma_start(out=cols[:, 160:192], in_=dfcb), s_ld, 16)
        R("sync", lambda e: e.dma_start(out=cols[:, 192:288], in_=dfcw), s_ld, 16)
        R("sync", lambda e: e.dma_start(out=BC[:], in_=dbc), s_ld, 16)
    ld_base = s_ld.n

    R("gpsimd", lambda e: e.memset(ident[:], 1.0), s_g, 16)
    W("gpsimd", s_g)
    R("gpsimd", lambda e: e.affine_select(
        out=ident[:], in_=ident[:], pattern=[[1, 128]],
        compare_op=ALU.is_equal, fill=0.0, base=0, channel_multiplier=-1), s_g, 1)
    R("gpsimd", lambda e: e.memset(ones1[:], 1.0), s_g, 16)
    R("gpsimd", lambda e: e.memset(ones32[:], 1.0), s_g, 16)
    g_const = s_g.n

    W("scalar", s_ld, ld_base)
    R("scalar", lambda e: e.activation(out=A_sb[:], in_=A_sb[:], func=ACT.Exp), s_s, 1)
    W("vector", s_s)
    R("vector", lambda e: e.tensor_scalar(
        out=A_sb[:], in0=A_sb[:], scalar1=-1.0, scalar2=None, op0=ALU.mult), s_v, 1)
    W("vector", s_ld, ld_base)
    R("vector", lambda e: e.tensor_scalar(
        out=cols[:, 290:291], in0=cols[:, 130:131], scalar1=B52, scalar2=None,
        op0=ALU.add), s_v, 1)
    W("vector", s_v)
    R("vector", lambda e: e.tensor_scalar(
        out=cols[:, 290:291], in0=cols[:, 290:291], scalar1=-B52, scalar2=None,
        op0=ALU.add), s_v, 1)
    W("vector", s_v)
    R("vector", lambda e: e.tensor_scalar(
        out=cols[:, 291:292], in0=cols[:, 290:291], scalar1=-1.0, scalar2=1.0,
        op0=ALU.mult, op1=ALU.add), s_v, 1)
    W("vector", s_g, g_const)
    R("vector", lambda e: e.tensor_copy(out=identb[:], in_=ident[:]), s_v, 1)
    R("vector", lambda e: e.tensor_copy(out=identf[:], in_=ident[:]), s_v, 1)
    for cc, val in ((960, EPS_LN1), (961, -0.5), (962, 0.5), (963, 1.0),
                    (964, -1.0), (965, float(D) / (D - 1)),
                    (966, 1.0 / (DH * D))):
        R("vector", lambda e, cc=cc, val=val: e.memset(
            cols[:, cc:cc + 1], val), s_v, 16)
    W("vector", s_v)
    c_eps = cols[:, 960:961]
    c_mh = cols[:, 961:962]
    c_ph = cols[:, 962:963]
    c_one = cols[:, 963:964]
    c_m1 = cols[:, 964:965]
    c_unb = cols[:, 965:966]
    c_mean = cols[:, 966:967]
    v_const = s_v.n

    def load_w_bf16(dst, dst_off, src_ap, kcols):
        W("sync", s_v, s_v.n)
        R("sync", lambda e, src_ap=src_ap, kcols=kcols: e.dma_start(
            out=work[:, 0:kcols], in_=src_ap), s_w, 16)
        W("vector", s_w)
        R("vector", lambda e, dst=dst, dst_off=dst_off, kcols=kcols: e.tensor_copy(
            out=dst[:, dst_off:dst_off + kcols], in_=work[:, 0:kcols]), s_v, 1)

    if summary_only:
        W("vector", s_ld, ld_base)
        for t in range(NT):
            ln_stats(lambda a, z, t=t: x_sb[:, t * D + a:t * D + z], 320 + 4 * t)
        v_stats = s_v.n
        W("scalar", s_v, v_stats)
        for t in range(NT):
            R("scalar", lambda e, t=t: e.activation(
                out=cols[:, 322 + 4 * t:323 + 4 * t],
                in_=cols[:, 321 + 4 * t:322 + 4 * t], func=ACT.Ln, bias=c_eps), s_s, 1)
        W("scalar", s_s)
        for t in range(NT):
            R("scalar", lambda e, t=t: e.activation(
                out=cols[:, 322 + 4 * t:323 + 4 * t],
                in_=cols[:, 322 + 4 * t:323 + 4 * t], func=ACT.Exp, scale=c_mh), s_s, 1)
        s_rsd = s_s.n
        W("vector", s_s, s_rsd)
        W("vector", s_v, v_stats)
        for t in range(NT):
            R("vector", lambda e, t=t: e.tensor_scalar(
                out=x_sb[:, t * D:(t + 1) * D], in0=x_sb[:, t * D:(t + 1) * D],
                scalar1=cols[:, 320 + 4 * t:321 + 4 * t],
                scalar2=cols[:, 322 + 4 * t:323 + 4 * t],
                op0=ALU.subtract, op1=ALU.mult), s_v, 1)
        v_ln1 = s_v.n

        W("tensor", s_v, v_ln1)
        W("tensor", s_g, g_const)
        for r in range(8):
            for t0 in range(0, NT, 4):
                tb = min(4, NT - t0)
                mm_group(ps_t, [
                    (lambda e, r=r, t=t0 + i, i=i: e.transpose(
                        ps_t.t[:, i * 128:(i + 1) * 128],
                        x_sb[:, t * D + r * 128:t * D + (r + 1) * 128],
                        ident[:]))
                    for i in range(tb)])
                evac(ps_t, lambda e, r=r, t0=t0, tb=tb: e.activation(
                    out=ln1T[:, r * TW + t0 * 128:r * TW + (t0 + tb) * 128],
                    in_=ps_t.t[:, 0:tb * 128], func=ACT.Identity))
        ev_ln1T = s_ev.n

        W("sync", s_ev, ev_ln1T)

        for r in range(8):
            load_w_bf16(wst, r * DI, dWin[r * 128:(r + 1) * 128, 0:DI], DI)
        CHW = chunks(0, TW, 272)

        def win_phase(j, zvariant):
            W("tensor", s_v, s_v.n)
            for ci, (lo, hi) in enumerate(CHW if not zvariant else CHY):
                pb = ps_a if ci % 2 == 0 else ps_b
                mm_group(pb, [
                    (lambda e, K=K, lo=lo, hi=hi, pb=pb, j=j: e.matmul(
                        pb.t[:, 0:hi - lo],
                        wst[:, K * DI + j * 128:K * DI + (j + 1) * 128],
                        ln1T[:, K * TW + lo:K * TW + hi],
                        start=(K == 0), stop=(K == 7)))
                    for K in range(8)])
                if not zvariant:
                    evac(pb, lambda e, lo=lo, hi=hi, pb=pb, j=j: e.activation(
                        out=p3x[:, lo:hi], in_=pb.t[:, 0:hi - lo],
                        func=ACT.Identity, bias=cols[:, j:j + 1]))
                else:
                    evac(pb, lambda e, lo=lo, hi=hi, pb=pb, j=j: e.activation(
                        out=p3c[:, (lo - YC0):(hi - YC0)],
                        in_=pb.t[:, 0:hi - lo],
                        func=ACT.Silu, bias=cols[:, 16 + j:17 + j]))

        for j in range(16):
            win_phase(j, False)
            ev_xin = s_ev.n
            W("vector", s_ev, ev_xin)
            W("vector", s_v, v_const)
            R("vector", lambda e: e.tensor_scalar(
                out=p3x[:, 0:4], in0=p3xf[:, 0:4],
                scalar1=cols[:, 128:129], scalar2=None, op0=ALU.mult), s_v, 1)
            for k in range(4):
                R("vector", lambda e, k=k, j=j: e.tensor_scalar(
                    out=pdiag[:, k * 128:(k + 1) * 128],
                    in0=ident[:], scalar1=cols[:, 64 + j * 4 + k:65 + j * 4 + k],
                    scalar2=None, op0=ALU.mult), s_v, 1)
            v_dg = s_v.n
            W("tensor", s_v, v_dg)
            W("tensor", s_ev, ps_w.mark)
            for ci, (lo, hi) in enumerate(CHY):
                n = hi - lo
                for k in range(4):
                    R("tensor", lambda e, k=k, lo=lo, n=n, ci=ci: e.matmul(
                        ps_w.t[:, ci * 512:ci * 512 + n],
                        pdiag[:, k * 128:(k + 1) * 128],
                        p3x[:, lo - 3 + k:lo - 3 + k + n],
                        start=(k == 0), stop=(k == 3)),
                        s_pv if (ci == len(CHY) - 1 and k == 3) else None, 1)
            W("scalar", s_v, s_v.n)
            W("scalar", s_sp, s_sp.n)
            W("scalar", s_pv)
            for ci, (lo, hi) in enumerate(CHY):
                n = hi - lo
                R("scalar", lambda e, ci=ci, lo=lo, n=n, j=j: e.activation(
                    out=p3c[:, (lo - YC0):(lo - YC0) + n],
                    in_=ps_w.t[:, ci * 512:ci * 512 + n],
                    func=ACT.Silu, bias=cols[:, 48 + j:49 + j]), s_ev, 1)
            ps_w.mark = s_ev.n
            W("vector", s_ev, s_ev.n)
            R("vector", lambda e, j=j: e.tensor_copy(
                out=xcb[:, j * NY:(j + 1) * NY],
                in_=p3c[:]), s_v, 1)
            W("gpsimd", s_ev, s_ev.n)
            R("gpsimd", lambda e, j=j: e.dma_start(
                out=dxcv[j * 128:(j + 1) * 128, :],
                in_=p3c[:]), s_sp, 16)
        v_xcb = s_v.n


    if summary_only:
        W("sync", s_v, v_xcb)
        for r in range(8):
            load_w_bf16(wst, r * DI, dWin[r * 128:(r + 1) * 128, DI:2 * DI], DI)
        for j in range(16):
            W("scalar", s_sp, s_sp.n)
            win_phase(j, True)
            W("gpsimd", s_ev, s_ev.n)
            R("gpsimd", lambda e, j=j: e.dma_start(
                out=dzsp[j * 128:(j + 1) * 128, :],
                in_=p3c[:]), s_sp, 16)

    if summary_only:
        W("sync", s_sp, s_sp.n)
        for r in range(16):
            load_w_bf16(xpst, r * 32, dWxp[r * 128:(r + 1) * 128, :], 32)
        v_xpst = s_v.n
        W("tensor", s_v, v_xpst)
        for ci, (lo, hi) in enumerate(CHY):
            n = hi - lo
            pb = ps_a if ci % 2 == 0 else ps_b
            mm_group(pb, [
                (lambda e, K=K, lo=lo, n=n, pb=pb: e.matmul(
                    pb.t[0:32, 0:n],
                    xpst[:, K * 32:(K + 1) * 32],
                    xcb[:, K * NY + (lo - YC0):K * NY + (lo - YC0) + n],
                    start=(K == 0), stop=(K == 15)))
                for K in range(16)])
            evac(pb, lambda e, lo=lo, n=n, pb=pb: e.activation(
                out=BCr[:, (lo - YC0):(lo - YC0) + n], in_=pb.t[0:32, 0:n],
                func=ACT.Identity, bias=colw[:, 0:1]))
        ev_bc = s_ev.n
        W("gpsimd", s_ev, ev_bc)
        R("gpsimd", lambda e: e.dma_start(out=dbc, in_=BC[:]), s_out, 16)


    if not summary_only:
        W("vector", s_v, v_const)
        R("vector", lambda e: e.memset(sumh[:], 0.0), s_v, 16)
        W("vector", s_v)
        W("sync", s_v, s_v.n)
        R("sync", lambda e: e.dma_start(
            out=work[:, 2048:2048 + 7 * 256], in_=dh0all[:, 0:7 * 256]), s_rs, 16)
        R("sync", lambda e: e.dma_start(
            out=work[:, 3840:3840 + 7 * 16], in_=ddsall[:, 0:7 * 16]), s_rs, 16)
        W("vector", s_rs)
        for m in range(7):
            R("vector", lambda e, m=m: e.tensor_scalar(
                out=cols[:, 400:416],
                in0=work[:, 3840 + m * 16:3840 + (m + 1) * 16],
                scalar1=cols[:, 131 + m:132 + m], scalar2=None,
                op0=ALU.mult), s_v, 1)
            W("vector", s_v)
            W("scalar", s_v)
            for j in range(16):
                R("scalar", lambda e, j=j: e.activation(
                    out=cols[:, 416 + j * 16:432 + j * 16],
                    in_=A_sb[:, j * 16:(j + 1) * 16],
                    func=ACT.Exp, scale=cols[:, 400 + j:401 + j]), s_s, 1)
            W("vector", s_s)
            R("vector", lambda e: e.tensor_tensor(
                out=work[:, 3968:4224], in0=sumh[:], in1=cols[:, 416:672],
                op=ALU.mult), s_v, 1)
            W("vector", s_v)
            R("vector", lambda e, m=m: e.scalar_tensor_tensor(
                out=sumh[:], in0=work[:, 2048 + m * 256:2048 + (m + 1) * 256],
                scalar=cols[:, 131 + m:132 + m], in1=work[:, 3968:4224],
                op0=ALU.mult, op1=ALU.add), s_v, 1)
            W("vector", s_v)

    if summary_only:
        WPAN = [wpanA, wpanB]
        WDST = [wdtst, wdtst2]

        def pan_dma(e, jp, slot, half):
            j = jp * 2 + half
            return e.dma_start(
                out=WPAN[slot][:, half * 2048:half * 2048 + 2048].rearrange(
                    "p (K c) -> p K c", c=128),
                in_=dWdt[:, j * 128:(j + 1) * 128].rearrange(
                    "(K p) c -> p K c", p=128))

        wcopy_hist = [0] * 8
        wmm_hist = [0] * 16
        pan_w_hist = [0] * 8
        PSEM = [s_wA, s_wB]
        W("sync", s_v, s_v.n)
        R("sync", lambda e: pan_dma(e, 0, 0, 0), PSEM[0], 16)
        R("sync", lambda e: pan_dma(e, 0, 0, 1), PSEM[0], 16)
        pan_w_hist[0] = PSEM[0].n
        for j in range(16):
            jp = j // 2
            if j % 2 == 0 and jp + 1 < 8:
                if jp >= 1:
                    W("sync", s_v, wcopy_hist[jp - 1])
                R("sync", lambda e, jp=jp: pan_dma(e, jp + 1, (jp + 1) % 2, 0),
                  PSEM[(jp + 1) % 2], 16)
                R("sync", lambda e, jp=jp: pan_dma(e, jp + 1, (jp + 1) % 2, 1),
                  PSEM[(jp + 1) % 2], 16)
                pan_w_hist[jp + 1] = PSEM[(jp + 1) % 2].n
            W("vector", PSEM[jp % 2], pan_w_hist[jp])
            if j >= 2:
                W("vector", s_pv, wmm_hist[j - 2])
            R("vector", lambda e, j=j, jp=jp: e.tensor_copy(
                out=WDST[j % 2][:],
                in_=WPAN[jp % 2][:, (j % 2) * 2048:(j % 2) * 2048 + 2048]), s_v, 1)
            wcopy_hist[jp] = s_v.n
            v_st = s_v.n
            W("tensor", s_v, v_st)
            W("tensor", s_ev, ps_w.mark)
            for K in range(16):
                for ci, (lo, hi) in enumerate(CHY):
                    n = hi - lo
                    R("tensor", lambda e, K=K, ci=ci, lo=lo, n=n, j=j: e.matmul(
                        ps_w.t[:, ci * 512:ci * 512 + n],
                        WDST[j % 2][:, K * 128:(K + 1) * 128],
                        xcb[:, K * NY + (lo - YC0):K * NY + (lo - YC0) + n],
                        start=(K == 0), stop=(K == 15)),
                        s_pv if (K == 15 and ci == len(CHY) - 1) else None, 1)
            wmm_hist[j] = s_pv.n
            W("scalar", s_v, s_v.n)
            W("scalar", s_sp, s_sp.n)
            W("scalar", s_pv)
            for ci, (lo, hi) in enumerate(CHY):
                n = hi - lo
                R("scalar", lambda e, ci=ci, lo=lo, n=n, j=j: e.activation(
                    out=dj2_w[:, (lo - YC0):(lo - YC0) + n],
                    in_=ps_w.t[:, ci * 512:ci * 512 + n],
                    func=ACT.Identity, bias=cols[:, 32 + j:33 + j]), s_ev, 1)
            ps_w.mark = s_ev.n
            W("scalar", s_ev, s_ev.n)
            R("scalar", lambda e: e.activation(out=dj_w[:], in_=dj2_w[:],
                                               func=ACT.Abs), s_s, 1)
            W("scalar", s_s)
            R("scalar", lambda e: e.activation(out=dj_w[:], in_=dj_w[:],
                                               func=ACT.Exp, scale=c_m1), s_s, 1)
            W("scalar", s_s)
            R("scalar", lambda e: e.activation(out=dj_w[:], in_=dj_w[:],
                                               func=ACT.Ln, bias=c_one), s_s, 1)
            R("scalar", lambda e: e.activation(out=dj2_w[:], in_=dj2_w[:],
                                               func=ACT.Relu), s_s, 1)
            W("vector", s_s, s_s.n)
            R("vector", lambda e: e.tensor_tensor(
                out=dj_w[:], in0=dj_w[:], in1=dj2_w[:], op=ALU.add), s_v, 1)
            if summary_only:
                W("vector", s_v)
                R("vector", lambda e, j=j: e.tensor_reduce(
                    out=sumd[:, j:j + 1], in_=dj_w[:, 1:1 + NS1], axis=AX,
                    op=ALU.add), s_v, 1)
            W("gpsimd", s_v, s_v.n)
            R("gpsimd", lambda e, j=j: e.dma_start(
                out=ddel[j * 128:(j + 1) * 128, :], in_=dj_w[:]), s_sp, 16)


    if not summary_only:
        W("tensor", s_ld, ld_base)
        W("tensor", s_g, g_const)
        mm_group(ps_t, [lambda e: e.transpose(
            ps_t.t[0:1, 0:32], BC[:, 0:1], ident[0:32, 0:32])])
        evac(ps_t, lambda e: e.activation(
            out=cols[0:1, 688:720], in_=ps_t.t[0:1, 0:32], func=ACT.Identity))
        mm_group(ps_t, [lambda e: e.matmul(
            ps_t.t[:, 0:16], ones1[:], cols[0:1, 704:720],
            start=True, stop=True)])
        evac(ps_t, lambda e: e.activation(
            out=cols[:, 672:688], in_=ps_t.t[:, 0:16], func=ACT.Identity))

    NQ = NSC if not summary_only else NS1
    HALVES = [(1, 1025), (1025, 1 + NQ)]
    tts_hist = []
    mm_hist = []
    for hh in range(2):
        c0, c1 = HALVES[hh]
        n = c1 - c0
        nrep = 32 if not summary_only else 16
        W("scalar", s_sp, s_sp.n)
        W("scalar", s_g, s_g.n)
        for s in range(nrep):
            dstt = brepf if s < 16 else crepf
            W("vector", s_pv, s_pv.n)
            W("vector", s_g, g_const)
            R("vector", lambda e, s=s: e.tensor_scalar(
                out=selw[:], in0=ones32[:], scalar1=ident[0:32, s:s + 1],
                scalar2=None, op0=ALU.mult), s_v, 1)
            v_sel = s_v.n
            for ci, (lo, hi) in enumerate(chunks(c0, c1, 512)):
                pb = ps_a if (s * 4 + ci) % 2 == 0 else ps_b
                W("tensor", s_v, v_sel)
                if hi - lo >= 256 and summary_only:
                    mm_group(pb, [lambda e, lo=lo, hi=hi, pb=pb: e.matmul(
                        pb.t[:, 0:hi - lo], selw[:],
                        BCr[:, lo:hi], start=True, stop=True)])
                else:
                    mm_group(pb, [lambda e, lo=lo, hi=hi, pb=pb: e.matmul(
                        pb.t[:, 0:hi - lo], selwf[:],
                        BC[:, lo:hi], start=True, stop=True)])
                evac(pb, lambda e, s=s, lo=lo, hi=hi, pb=pb, c0=c0, dstt=dstt:
                     e.activation(out=dstt[:, (s % 16) * HW2 + (lo - c0):
                                            (s % 16) * HW2 + (hi - c0)],
                                  in_=pb.t[:, 0:hi - lo], func=ACT.Identity))
        ev_reps = s_ev.n

        def sc1ap(i, n):
            return sc1[:, 0:n] if i == 0 else work[:, 0:n]

        def sc2ap(i, n):
            return sc2[:, 0:n] if i == 0 else work[:, 1025:1025 + n]

        def sc3ap(i, n):
            return sc3[:, 0:n] if i == 0 else work[:, 3075:3075 + n]

        for j in range(16):
            W("sync", s_s, s_s.n)
            W("sync", s_v, s_v.n)
            W("sync", s_g, s_g.n)
            R("sync", lambda e, j=j, c0=c0, c1=c1: e.dma_start(
                out=djh[:, 0:c1 - c0],
                in_=ddel[j * 128:(j + 1) * 128, c0:c1]), s_rs, 16)
            R("sync", lambda e, j=j, c0=c0, c1=c1: e.dma_start(
                out=ujh[:, 0:c1 - c0],
                in_=dxcv[j * 128:(j + 1) * 128, c0:c1]), s_rs, 16)
            W("vector", s_rs)
            R("vector", lambda e, n=n: e.tensor_tensor(
                out=ujh[:, 0:n], in0=djh[:, 0:n], in1=ujh[:, 0:n],
                op=ALU.mult), s_v, 1)
            v_uj = s_v.n
            W("scalar", s_v, v_uj)
            W("gpsimd", s_v, v_uj)
            W("gpsimd", s_ev, ev_reps)
            if not summary_only:
                W("tensor", s_ev, ps_w.mark)
            sc1_ns = [0] * 16
            sc2_ns = [0] * 16

            jbase = len(tts_hist)

            def emit_sc1(s):
                k = jbase + s
                if k - 2 >= 0:
                    W("scalar", s_v, tts_hist[k - 2])
                R("scalar", lambda e, s=s, j=j, n=n: e.activation(
                    out=sc1ap(s % 2, n), in_=djh[:, 0:n], func=ACT.Exp,
                    scale=A_sb[:, j * 16 + s:j * 16 + s + 1]), s_s, 1)
                sc1_ns[s] = s_s.n

            def emit_sc2(s):
                k = jbase + s
                if k - 2 >= 0:
                    W("gpsimd", s_v, tts_hist[k - 2])
                R("gpsimd", lambda e, s=s, n=n: e.tensor_tensor(
                    out=sc2ap(s % 2, n), in0=ujh[:, 0:n],
                    in1=brepf[:, s * HW2:s * HW2 + n], op=ALU.mult), s_g, 1)
                sc2_ns[s] = s_g.n

            for s in range(16):
                emit_sc1(s)
                if s == 0:
                    emit_sc2(0)
                if s + 1 < 16:
                    emit_sc2(s + 1)
                W("vector", s_s, sc1_ns[s])
                W("vector", s_g, sc2_ns[s])
                W("vector", s_v)
                if hh == 0:
                    if not summary_only:
                        R("vector", lambda e, s=s, j=j, n=n: e.tensor_tensor_scan(
                            out=sc3ap(s % 2, n), data0=sc1ap(s % 2, n),
                            data1=sc2ap(s % 2, n),
                            initial=sumh[:, j * 16 + s:j * 16 + s + 1],
                            op0=ALU.mult, op1=ALU.add), s_v, 1)
                    else:
                        R("vector", lambda e, s=s, j=j, n=n: e.tensor_tensor_scan(
                            out=sc3ap(s % 2, n), data0=sc1ap(s % 2, n),
                            data1=sc2ap(s % 2, n),
                            initial=0.0, op0=ALU.mult, op1=ALU.add), s_v, 1)
                    tts_hist.append(s_v.n)
                    W("vector", s_v)
                    R("vector", lambda e, s=s, j=j, n=n: e.tensor_copy(
                        out=hcar[:, j * 16 + s:j * 16 + s + 1],
                        in_=sc3ap(s % 2, n)[:, n - 1:n]), s_v, 1)
                else:
                    R("vector", lambda e, s=s, j=j, n=n: e.tensor_tensor_scan(
                        out=sc3ap(s % 2, n), data0=sc1ap(s % 2, n),
                        data1=sc2ap(s % 2, n),
                        initial=hcar[:, j * 16 + s:j * 16 + s + 1],
                        op0=ALU.mult, op1=ALU.add), s_v, 1)
                    tts_hist.append(s_v.n)
                    W("vector", s_v)
                    if summary_only:
                        R("vector", lambda e, s=s, j=j, n=n: e.tensor_copy(
                            out=sumh[:, j * 16 + s:j * 16 + s + 1],
                            in_=sc3ap(s % 2, n)[:, n - 1:n]), s_v, 1)
                if not summary_only:
                    on_v = (s % 4 == 0)
                    if on_v:
                        if mm_hist:
                            W("vector", s_pv, mm_hist[-1])
                        R("vector", lambda e, s=s, n=n: e.tensor_tensor(
                            out=sc4[:, 0:n], in0=sc3ap(s % 2, n),
                            in1=crepf[:, s * HW2:s * HW2 + n], op=ALU.mult),
                          s_v, 1)
                        W("tensor", s_v)
                    else:
                        W("gpsimd", s_v, tts_hist[-1])
                        if mm_hist:
                            W("gpsimd", s_pv, mm_hist[-1])
                        R("gpsimd", lambda e, s=s, n=n: e.tensor_tensor(
                            out=sc4[:, 0:n], in0=sc3ap(s % 2, n),
                            in1=crepf[:, s * HW2:s * HW2 + n], op=ALU.mult),
                          s_g, 1)
                        W("tensor", s_g, s_g.n)
                    for ci, (lo, hi) in enumerate(chunks(0, n, 512)):
                        if hi - lo >= 256:
                            R("tensor", lambda e, ci=ci, lo=lo, hi=hi, s=s: e.matmul(
                                ps_w.t[:, ci * 512:ci * 512 + (hi - lo)], identf[:],
                                sc4[:, lo:hi], start=(s == 0), stop=(s == 15)),
                                s_pv if hi == n else None, 1)
                        else:
                            R("tensor", lambda e, ci=ci, lo=lo, hi=hi, s=s: e.matmul(
                                ps_w.t[:, ci * 512:ci * 512 + (hi - lo)], ident[:],
                                sc4f[:, lo:hi], start=(s == 0), stop=(s == 15)),
                                s_pv if hi == n else None, 1)
                    mm_hist.append(s_pv.n)
            if not summary_only:
                W("scalar", s_sp, s_sp.n)
                W("scalar", s_pv)
                for ci, (lo, hi) in enumerate(chunks(0, n, 512)):
                    R("scalar", lambda e, ci=ci, lo=lo, hi=hi: e.activation(
                        out=work[:, 2050 + lo:2050 + hi],
                        in_=ps_w.t[:, ci * 512:ci * 512 + (hi - lo)],
                        func=ACT.Identity), s_ev, 1)
                ps_w.mark = s_ev.n
                W("gpsimd", s_ev, s_ev.n)
                R("gpsimd", lambda e, j=j, c0=c0, c1=c1, n=n: e.dma_start(
                    out=dysp[j * 128:(j + 1) * 128, c0:c1],
                    in_=work[:, 2050:2050 + n]), s_sp, 16)
        if hh == 0:
            W("vector", s_v, s_v.n)

    if summary_only:
        W("gpsimd", s_v, s_v.n)
        R("gpsimd", lambda e: e.dma_start(out=dh0, in_=sumh[:]), s_out, 16)
        R("gpsimd", lambda e: e.dma_start(out=dds, in_=sumd[:]), s_out, 16)
        W("gpsimd", s_out, s_out.n)
        W("gpsimd", s_sp, s_sp.n)
        return b

    W("sync", s_sp, s_sp.n)
    W("vector", s_ev, s_ev.n)
    YG = [yg, yg2]
    UG = [ug, ug2]
    DG = [dg, dg2]
    YGB = [ygb, ygb2]
    g_hist = [0] * 16
    g_use = [0] * 16
    g_spl = [0] * 16

    def g_loads(j, psem):
        R("sync", lambda e, j=j: e.dma_start(
            out=YG[j % 2][:, 1:NY],
            in_=dysp[j * 128:(j + 1) * 128, 1:NY]), psem, 16)
        R("sync", lambda e, j=j: e.dma_start(
            out=UG[j % 2][:], in_=dxcv[j * 128:(j + 1) * 128, :]), psem, 16)
        R("sync", lambda e, j=j: e.dma_start(
            out=DG[j % 2][:], in_=dzsp[j * 128:(j + 1) * 128, :]), psem, 16)
        g_hist[j] = psem.n

    W("sync", s_v, s_v.n)
    W("sync", s_s, s_s.n)
    g_loads(0, s_wA)
    for j in range(16):
        if j + 1 < 16:
            if j >= 1:
                W("sync", s_v, g_use[j - 1])
            g_loads(j + 1, s_wA if (j + 1) % 2 == 0 else s_wB)
        W("vector", s_wA if j % 2 == 0 else s_wB, g_hist[j])
        R("vector", lambda e, j=j: e.tensor_tensor(
            out=cols[:, 704:720], in0=sumh[:, j * 16:(j + 1) * 16],
            in1=cols[:, 672:688], op=ALU.mult), s_v, 1)
        W("vector", s_v)
        R("vector", lambda e, j=j: e.tensor_reduce(
            out=YG[j % 2][:, 0:1], in_=cols[:, 704:720], axis=AX,
            op=ALU.add), s_v, 1)
        W("vector", s_v)
        R("vector", lambda e, j=j: e.scalar_tensor_tensor(
            out=YG[j % 2][:], in0=UG[j % 2][:],
            scalar=cols[:, 144 + j:145 + j], in1=YG[j % 2][:],
            op0=ALU.mult, op1=ALU.add), s_v, 1)
        W("vector", s_v)
        R("vector", lambda e, j=j: e.tensor_tensor(
            out=YG[j % 2][:], in0=YG[j % 2][:], in1=DG[j % 2][:],
            op=ALU.mult), s_v, 1)
        W("vector", s_v)
        if j >= 2:
            W("vector", s_spA if j % 2 == 0 else s_spB, g_spl[j - 2])
        R("vector", lambda e, j=j: e.tensor_copy(
            out=YGB[j % 2][:], in_=YG[j % 2][:]), s_v, 1)
        g_use[j] = s_v.n
        W("gpsimd", s_v, s_v.n)
        spsem = s_spA if j % 2 == 0 else s_spB
        R("gpsimd", lambda e, j=j: e.dma_start(
            out=dgsp[j * 128:(j + 1) * 128, :], in_=YGB[j % 2][:]), spsem, 16)
        g_spl[j] = spsem.n
    sp_gA = s_spA.n
    sp_gB = s_spB.n
    sp_g = s_sp.n

    W("gpsimd", s_spA, sp_gA)
    W("gpsimd", s_spB, sp_gB)
    R("gpsimd", lambda e: e.memset(x2[:], 0.0), s_g, 16)
    g_x2 = s_g.n
    W("sync", s_spA, sp_gA)
    W("sync", s_spB, sp_gB)
    for r in range(16):
        load_w_bf16(woutst, r * 1024, dWout[r * 128:(r + 1) * 128, :], 1024)
    w_wout = s_w.n
    v_wout = s_v.n
    for rr in range(3):
        W("sync", s_pv, s_pv.n)
        W("sync", s_v, s_v.n)
        R("sync", lambda e, rr=rr: e.dma_start(
            out=work[0:1, 0:D], in_=dabb[rr:rr + 1, :]), s_ld, 16)
        W("tensor", s_ld, s_ld.n)
        for ci in range(2):
            pb = ps_a if ci % 2 == 0 else ps_b
            mm_group(pb, [lambda e, ci=ci, pb=pb: e.matmul(
                pb.t[:, 0:512], ones1[:],
                work[0:1, ci * 512:(ci + 1) * 512],
                start=True, stop=True)])
            evac(pb, lambda e, rr=rr, ci=ci, pb=pb: e.activation(
                out=abbr[:, rr * 1024 + ci * 512:rr * 1024 + (ci + 1) * 512],
                in_=pb.t[:, 0:512], func=ACT.Identity))
    ev_abb = s_ev.n

    W("tensor", s_w, w_wout)
    W("vector", s_g, g_x2)
    W("sync", s_ev, ev_abb)
    GST = [gst, gstB]
    wo_hist = [0] * NT
    wo_mm = [0] * NT
    wo_use = [0] * NT

    def wo_loads(t, psem):
        nr = min(NY, (t + 1) * 128) - t * 128
        for jj in range(16):
            R("sync", lambda e, jj=jj, t=t, nr=nr: e.dma_start(
                out=GST[t % 2][:, jj * 128:jj * 128 + nr],
                in_=dgsp[jj * 128:(jj + 1) * 128,
                         t * 128:t * 128 + nr]), psem, 16)
        R("sync", lambda e, t=t, nr=nr: e.dma_start(
            out=work[0:nr, (t % 2) * 3072:(t % 2) * 3072 + D],
            in_=dx[t * 128 + 3:t * 128 + 3 + nr, :]), psem, 16)
        wo_hist[t] = psem.n

    W("sync", s_v, s_v.n)
    W("sync", s_pv, s_pv.n)
    wo_loads(0, s_wA)
    for tq in range(NT):
        nrow = min(NY, (tq + 1) * 128) - tq * 128
        q0 = tq * 128
        xw = (tq % 2) * 3072
        if tq + 1 < NT:
            if tq >= 1:
                W("sync", s_pv, wo_mm[tq - 1])
                W("sync", s_v, wo_use[tq - 1])
            wo_loads(tq + 1, s_wA if (tq + 1) % 2 == 0 else s_wB)
        W("vector", s_wA if tq % 2 == 0 else s_wB, wo_hist[tq])
        ln_stats(lambda a, z, nrow=nrow, xw=xw: work[0:nrow, xw + a:xw + z],
                 740, 0, nrow)
        W("scalar", s_v)
        R("scalar", lambda e, nrow=nrow: e.activation(
            out=cols[0:nrow, 742:743], in_=cols[0:nrow, 741:742],
            func=ACT.Ln, bias=cols[0:nrow, 960:961]), s_s, 1)
        W("scalar", s_s)
        R("scalar", lambda e, nrow=nrow: e.activation(
            out=cols[0:nrow, 742:743], in_=cols[0:nrow, 742:743],
            func=ACT.Exp, scale=cols[0:nrow, 961:962]), s_s, 1)
        W("vector", s_s)
        R("vector", lambda e, nrow=nrow, xw=xw: e.tensor_scalar(
            out=work[0:nrow, 1024:2048], in0=work[0:nrow, xw:xw + D],
            scalar1=cols[0:nrow, 740:741],
            scalar2=cols[0:nrow, 742:743],
            op0=ALU.subtract, op1=ALU.mult), s_v, 1)
        v_lnrec = s_v.n
        W("tensor", s_v, v_lnrec)
        for ci in range(2):
            pb = ps_a if ci % 2 == 0 else ps_b
            mm_group(pb, [
                (lambda e, jj=jj, ci=ci, pb=pb, nrow=nrow, tq=tq: e.matmul(
                    pb.t[0:nrow, 0:512],
                    GST[tq % 2][:, jj * 128:jj * 128 + nrow],
                    woutst[:, jj * 1024 + ci * 512:jj * 1024 + (ci + 1) * 512],
                    start=(jj == 0), stop=(jj == 15)))
                for jj in range(16)])
            W("vector", s_pv)
            W("vector", s_ev, ev_abb)
            R("vector", lambda e, ci=ci, pb=pb, nrow=nrow: e.tensor_tensor(
                out=work[0:nrow, 2048 + ci * 512:2048 + (ci + 1) * 512],
                in0=pb.t[0:nrow, 0:512],
                in1=work[0:nrow, 1024 + ci * 512:1024 + (ci + 1) * 512],
                op=ALU.add), s_v, 1)
            W("vector", s_v)
            R("vector", lambda e, ci=ci, nrow=nrow: e.tensor_tensor(
                out=work[0:nrow, 2048 + ci * 512:2048 + (ci + 1) * 512],
                in0=work[0:nrow, 2048 + ci * 512:2048 + (ci + 1) * 512],
                in1=abbr[0:nrow, 2048 + ci * 512:2048 + (ci + 1) * 512],
                op=ALU.add), s_v, 1)
            R("vector", lambda e: e.engine_nop(), s_ev, 1)
            pb.mark = s_ev.n
        wo_mm[tq] = s_pv.n
        W("vector", s_v)
        ln_stats(lambda a, z, nrow=nrow: work[0:nrow, 2048 + a:2048 + z],
                 312, 0, nrow)
        W("scalar", s_v)
        R("scalar", lambda e, nrow=nrow: e.activation(
            out=cols[0:nrow, 314:315], in_=cols[0:nrow, 313:314],
            func=ACT.Ln, scale=cols[0:nrow, 965:966]), s_s, 1)
        W("scalar", s_s)
        R("scalar", lambda e, nrow=nrow: e.activation(
            out=cols[0:nrow, 314:315], in_=cols[0:nrow, 314:315],
            func=ACT.Exp, scale=cols[0:nrow, 962:963]), s_s, 1)
        W("vector", s_s)
        R("vector", lambda e, nrow=nrow: e.tensor_scalar(
            out=cols[0:nrow, 315:316], in0=cols[0:nrow, 314:315],
            scalar1=EPS_IN, scalar2=None, op0=ALU.add), s_v, 1)
        W("vector", s_v)
        R("vector", lambda e, nrow=nrow: e.reciprocal(
            out=cols[0:nrow, 316:317], in_=cols[0:nrow, 315:316]), s_v, 1)
        W("vector", s_v)
        R("vector", lambda e, nrow=nrow: e.tensor_scalar(
            out=work[0:nrow, 2048:3072], in0=work[0:nrow, 2048:3072],
            scalar1=cols[0:nrow, 312:313], scalar2=cols[0:nrow, 316:317],
            op0=ALU.subtract, op1=ALU.mult), s_v, 1)
        W("vector", s_v)
        R("vector", lambda e, nrow=nrow: e.tensor_tensor(
            out=work[0:nrow, 2048:3072], in0=work[0:nrow, 2048:3072],
            in1=abbr[0:nrow, 0:1024], op=ALU.mult), s_v, 1)
        W("vector", s_v)
        R("vector", lambda e, nrow=nrow: e.tensor_tensor(
            out=work[0:nrow, 2048:3072], in0=work[0:nrow, 2048:3072],
            in1=abbr[0:nrow, 1024:2048], op=ALU.add), s_v, 1)
        W("vector", s_v)
        R("vector", lambda e, tq=tq, nrow=nrow, xw=xw: e.tensor_tensor(
            out=x2[0:nrow, tq * D:(tq + 1) * D], in0=work[0:nrow, 2048:3072],
            in1=work[0:nrow, xw:xw + D], op=ALU.add), s_v, 1)
        wo_use[tq] = s_v.n
        W("vector", s_v)
    v_xp2 = s_v.n
    W("gpsimd", s_v, v_xp2)
    for t in range(NT):
        R("gpsimd", lambda e, t=t: e.dma_start(
            out=dxp[t * 128:(t + 1) * 128, :],
            in_=x2[:, t * D:(t + 1) * D]), s_sp, 16)

    for tq in range(NT):
        ln_stats(lambda a, z, tq=tq: x2[:, tq * D + a:tq * D + z], 740)
        W("scalar", s_v)
        R("scalar", lambda e: e.activation(
            out=cols[:, 742:743], in_=cols[:, 741:742],
            func=ACT.Ln, bias=c_eps), s_s, 1)
        W("scalar", s_s)
        R("scalar", lambda e: e.activation(
            out=cols[:, 742:743], in_=cols[:, 742:743],
            func=ACT.Exp, scale=c_mh), s_s, 1)
        W("vector", s_s)
        R("vector", lambda e, tq=tq: e.tensor_scalar(
            out=work[:, 0:D], in0=x2[:, tq * D:(tq + 1) * D],
            scalar1=cols[:, 740:741], scalar2=cols[:, 742:743],
            op0=ALU.subtract, op1=ALU.mult), s_v, 1)
        W("vector", s_v)
        R("vector", lambda e: e.tensor_reduce(
            out=cols[:, 296:297], in_=work[:, 0:D], axis=AX, op=ALU.max,
            apply_absolute_value=True), s_v, 1)
        W("vector", s_v)
        R("vector", lambda e, tq=tq: e.tensor_scalar(
            out=cols[:, 440 + 2 * tq:441 + 2 * tq], in0=cols[:, 296:297],
            scalar1=1e-5, scalar2=None, op0=ALU.max), s_v, 1)
        W("vector", s_v)
        R("vector", lambda e, tq=tq: e.reciprocal(
            out=cols[:, 297:298], in_=cols[:, 440 + 2 * tq:441 + 2 * tq]), s_v, 1)
        W("vector", s_v)
        R("vector", lambda e, tq=tq: e.tensor_scalar(
            out=cols[:, 441 + 2 * tq:442 + 2 * tq], in0=cols[:, 297:298],
            scalar1=127.0, scalar2=None, op0=ALU.mult), s_v, 1)
        W("vector", s_v)
        emit_round(lambda tq=tq: xq[:, tq * D:(tq + 1) * D],
                   lambda: work[:, 0:D], cols[:, 441 + 2 * tq:442 + 2 * tq],
                   lambda: work[:, 1024:2048], lambda: work[:, 2048:3072])
        W("vector", s_v)
    v_xq = s_v.n
    W("tensor", s_v, v_xq)
    for r in range(8):
        for t0 in range(0, NT, 4):
            tb = min(4, NT - t0)
            mm_group(ps_t, [
                (lambda e, r=r, t=t0 + i, i=i: e.transpose(
                    ps_tb[:, i * 128:(i + 1) * 128],
                    xq[:, t * D + r * 128:t * D + (r + 1) * 128],
                    identb[:]))
                for i in range(tb)])
            evac(ps_t, lambda e, r=r, t0=t0, tb=tb: e.tensor_copy(
                out=xqT[:, r * TW + t0 * 128:r * TW + (t0 + tb) * 128],
                in_=ps_tb[:, 0:tb * 128]), eng="vector")
    ev_xqT = s_ev.n

    W("vector", s_sp, s_sp.n)

    def w1slot(i):
        return work[:, (i % 2) * 1024:(i % 2) * 1024 + 1024]

    w1_hist = [0] * 32
    w1_use = [0] * 32
    W("sync", s_v, s_v.n)
    R("sync", lambda e: e.dma_start(out=w1slot(0), in_=dw1[0:128, :]), s_wA, 16)
    w1_hist[0] = s_wA.n
    for i in range(32):
        if i + 1 < 32:
            if i >= 1:
                W("sync", s_v, w1_use[i - 1])
            psem = s_wA if (i + 1) % 2 == 0 else s_wB
            R("sync", lambda e, i=i: e.dma_start(
                out=w1slot(i + 1), in_=dw1[(i + 1) * 128:(i + 2) * 128, :]),
              psem, 16)
            w1_hist[i + 1] = psem.n
        W("vector", s_wA if i % 2 == 0 else s_wB, w1_hist[i])
        R("vector", lambda e, i=i: e.tensor_reduce(
            out=cols[:, 800 + i:801 + i], in_=w1slot(i), axis=AX, op=ALU.add,
            apply_absolute_value=True), s_v, 1)
        w1_use[i] = s_v.n
    W("vector", s_v)
    R("vector", lambda e: e.tensor_reduce(
        out=cols[:, 833:834], in_=cols[:, 800:832], axis=AX, op=ALU.add), s_v, 1)
    v_par = s_v.n
    W("tensor", s_v, v_par)
    mm_group(ps_t, [lambda e: e.matmul(
        ps_t.t[0:1, 0:1], cols[:, 833:834], cols[:, 130:131],
        start=True, stop=True)])
    evac(ps_t, lambda e: e.activation(out=cols[0:1, 834:835], in_=ps_t.t[0:1, 0:1],
                                      func=ACT.Identity,
                                      scale=cols[0:1, 966:967]))
    W("vector", s_ev, s_ev.n)
    R("vector", lambda e: e.tensor_scalar(
        out=cols[0:1, 834:835], in0=cols[0:1, 834:835],
        scalar1=1e-5, scalar2=None, op0=ALU.max), s_v, 1)
    v_mw = s_v.n
    W("tensor", s_v, v_mw)
    mm_group(ps_t, [lambda e: e.matmul(
        ps_t.t[:, 0:1], ones1[:], cols[0:1, 834:835], start=True, stop=True)])
    evac(ps_t, lambda e: e.activation(out=cols[:, 835:836], in_=ps_t.t[:, 0:1],
                                      func=ACT.Identity))
    W("vector", s_ev, s_ev.n)
    R("vector", lambda e: e.reciprocal(out=cols[:, 836:837],
                                       in_=cols[:, 835:836]), s_v, 1)
    W("vector", s_v)
    for tq in range(NT):
        R("vector", lambda e, tq=tq: e.tensor_scalar(
            out=cols[:, 480 + tq:481 + tq], in0=cols[:, 440 + 2 * tq:441 + 2 * tq],
            scalar1=cols[:, 835:836], scalar2=1.0 / 127.0,
            op0=ALU.mult, op1=ALU.mult), s_v, 1)
    v_f1 = s_v.n
    w1_hist = [0] * 32
    w1_use = [0] * 32
    W("sync", s_v, s_v.n)
    R("sync", lambda e: e.dma_start(out=w1slot(0), in_=dw1[0:128, :]), s_wA, 16)
    w1_hist[0] = s_wA.n
    for i in range(32):
        if i + 1 < 32:
            if i >= 1:
                W("sync", s_v, w1_use[i - 1])
            psem = s_wA if (i + 1) % 2 == 0 else s_wB
            R("sync", lambda e, i=i: e.dma_start(
                out=w1slot(i + 1), in_=dw1[(i + 1) * 128:(i + 2) * 128, :]),
              psem, 16)
            w1_hist[i + 1] = psem.n
        W("vector", s_wA if i % 2 == 0 else s_wB, w1_hist[i])
        emit_round(lambda: djq1[:], lambda i=i: w1slot(i), cols[:, 836:837],
                   lambda: work[:, 2048:3072], lambda: work[:, 3072:4096])
        w1_use[i] = s_v.n
        W("vector", s_v)
        R("vector", lambda e: e.tensor_scalar(
            out=djq1[:], in0=djq1[:],
            scalar1=-1.0, scalar2=1.0, op0=ALU.max, op1=ALU.min), s_v, 1)
        v_wq = s_v.n
        W("tensor", s_v, v_wq)
        for r0 in range(0, 8, 4):
            mm_group(ps_t, [
                (lambda e, r=r0 + i2, i2=i2: e.transpose(
                    ps_t.t[:, i2 * 128:(i2 + 1) * 128],
                    djq1[:, r * 128:(r + 1) * 128], ident[:]))
                for i2 in range(4)])
            W("vector", s_pv)
            for i2 in range(4):
                R("vector", lambda e, i=i, r=r0 + i2, i2=i2: e.tensor_copy(
                    out=w1qT[:, r * DH + i * 128:r * DH + (i + 1) * 128],
                    in_=ps_t.t[:, i2 * 128:(i2 + 1) * 128]), s_ev, 1)
            ps_t.mark = s_ev.n
    ev_w1q = s_ev.n

    W("scalar", s_v, v_f1)
    W("scalar", s_sp, s_sp.n)
    for g in range(16):
        W("tensor", s_ev, ev_w1q)
        W("tensor", s_ev, ev_xqT)
        for t in range(NT):
            pb = ps_a if t % 2 == 0 else ps_b
            mm_group(pb, [
                (lambda e, K=K, t=t, g=g, pb=pb: e.matmul(
                    pb.t[:, 0:GW],
                    xqT[:, K * TW + t * 128:K * TW + (t + 1) * 128],
                    w1qT[:, K * DH + g * GW:K * DH + (g + 1) * GW],
                    start=(K == 0), stop=(K == 7)))
                for K in range(8)])
            evac(pb, lambda e, t=t, pb=pb: e.activation(
                out=h1g[:, t * GW:(t + 1) * GW], in_=pb.t[:, 0:GW],
                func=ACT.Identity, scale=cols[:, 480 + t:481 + t]))
        ev_h1g = s_ev.n
        W("tensor", s_ev, ev_h1g)
        for jj in range(2):
            for t0 in range(0, NT, 4):
                tb = min(4, NT - t0)
                mm_group(ps_t, [
                    (lambda e, jj=jj, t=t0 + i, i=i: e.transpose(
                        ps_t.t[:, i * 128:(i + 1) * 128],
                        h1g[:, t * GW + jj * 128:t * GW + (jj + 1) * 128],
                        ident[:]))
                    for i in range(tb)])
                wdt = min(2056, (t0 + tb) * 128) - t0 * 128
                evac(ps_t, lambda e, jj=jj, t0=t0, wdt=wdt: e.activation(
                    out=h1Tr[:, jj * 2056 + t0 * 128:jj * 2056 + t0 * 128 + wdt],
                    in_=ps_t.t[:, 0:wdt], func=ACT.Identity))
        ev_h1T = s_ev.n
        for jj in range(2):
            jt = g * 2 + jj
            W("vector", s_ev, ev_h1T)
            R("vector", lambda e, jj=jj: e.tensor_scalar(
                out=h1Tr[:, jj * 2056:jj * 2056 + 1],
                in0=h1T[:, jj * 2056:jj * 2056 + 1],
                scalar1=cols[:, 128:129], scalar2=None, op0=ALU.mult), s_v, 1)
            R("vector", lambda e, jj=jj: e.tensor_scalar(
                out=h1Tr[:, jj * 2056 + 2049:jj * 2056 + 2050],
                in0=h1T[:, jj * 2056 + 2049:jj * 2056 + 2050],
                scalar1=cols[:, 129:130], scalar2=None, op0=ALU.mult), s_v, 1)
            W("vector", s_pv, s_pv.n)
            for k in range(3):
                R("vector", lambda e, k=k, jt=jt: e.tensor_scalar(
                    out=fdiag[:, k * 128:(k + 1) * 128],
                    in0=ident[:], scalar1=cols[:, 192 + jt * 3 + k:193 + jt * 3 + k],
                    scalar2=None, op0=ALU.mult), s_v, 1)
            v_cdg = s_v.n
            W("tensor", s_v, v_cdg)
            W("tensor", s_ev, ps_w.mark)
            for ci in range(4):
                for k in range(3):
                    R("tensor", lambda e, ci=ci, k=k, jj=jj: e.matmul(
                        ps_w.t[:, ci * 512:(ci + 1) * 512],
                        fdiag[:, k * 128:(k + 1) * 128],
                        h1Tr[:, jj * 2056 + ci * 512 + k:
                             jj * 2056 + ci * 512 + k + 512],
                        start=(k == 0), stop=(k == 2)),
                        s_pv if (ci == 3 and k == 2) else None, 1)
            W("scalar", s_sp, s_sp.n)
            W("scalar", s_pv)
            for ci in range(4):
                R("scalar", lambda e, ci=ci, jt=jt: e.activation(
                    out=hsj[:, ci * 512:(ci + 1) * 512],
                    in_=ps_w.t[:, ci * 512:(ci + 1) * 512],
                    func=ACT.Silu, bias=cols[:, 160 + jt:161 + jt]), s_ev, 1)
            ps_w.mark = s_ev.n
            W("tensor", s_ev, s_ev.n)
            for t0 in range(0, 16, 4):
                mm_group(ps_t, [
                    (lambda e, t=t0 + i, i=i: e.transpose(
                        ps_t.t[:, i * 128:(i + 1) * 128],
                        hsj[:, t * 128:(t + 1) * 128], ident[:]))
                    for i in range(4)])
                evac(ps_t, lambda e, t0=t0: e.activation(
                    out=hst[:, t0 * 128:(t0 + 4) * 128],
                    in_=ps_t.t[:, 0:512], func=ACT.Identity))
            ev_hst = s_ev.n
            W("gpsimd", s_ev, ev_hst)
            R("gpsimd", lambda e, jt=jt: e.dma_start(
                out=dhs[:, jt * 128:(jt + 1) * 128].rearrange(
                    "(t p) c -> p t c", p=128),
                in_=hst.ap().rearrange("p (t c) -> p t c", c=128)), s_sp, 16)

    W("sync", s_sp, s_sp.n)

    def w2half(hf):
        return work[:, (hf % 2) * 2048:(hf % 2) * 2048 + 2048]

    def w2src(hf):
        return dw2[(hf // 2) * 128:(hf // 2 + 1) * 128,
                   (hf % 2) * 2048:(hf % 2) * 2048 + 2048]

    w2_hist = [0] * 16
    w2_use = [0] * 16
    W("sync", s_v, s_v.n)
    R("sync", lambda e: e.dma_start(out=w2half(0), in_=w2src(0)), s_wA, 16)
    w2_hist[0] = s_wA.n
    for hf in range(16):
        if hf + 1 < 16:
            if hf >= 1:
                W("sync", s_v, w2_use[hf - 1])
            psem = s_wA if (hf + 1) % 2 == 0 else s_wB
            R("sync", lambda e, hf=hf: e.dma_start(
                out=w2half(hf + 1), in_=w2src(hf + 1)), psem, 16)
            w2_hist[hf + 1] = psem.n
        W("vector", s_wA if hf % 2 == 0 else s_wB, w2_hist[hf])
        R("vector", lambda e, hf=hf: e.tensor_reduce(
            out=cols[:, 840 + hf:841 + hf], in_=w2half(hf), axis=AX, op=ALU.add,
            apply_absolute_value=True), s_v, 1)
        w2_use[hf] = s_v.n
    W("vector", s_v)
    R("vector", lambda e: e.tensor_reduce(
        out=cols[:, 868:869], in_=cols[:, 840:856], axis=AX, op=ALU.add), s_v, 1)
    v_par2 = s_v.n
    W("tensor", s_v, v_par2)
    mm_group(ps_t, [lambda e: e.matmul(
        ps_t.t[0:1, 0:1], cols[:, 868:869], cols[:, 130:131],
        start=True, stop=True)])
    evac(ps_t, lambda e: e.activation(out=cols[0:1, 869:870], in_=ps_t.t[0:1, 0:1],
                                      func=ACT.Identity,
                                      scale=cols[0:1, 966:967]))
    W("vector", s_ev, s_ev.n)
    R("vector", lambda e: e.tensor_scalar(
        out=cols[0:1, 869:870], in0=cols[0:1, 869:870],
        scalar1=1e-5, scalar2=None, op0=ALU.max), s_v, 1)
    v_mw2 = s_v.n
    W("tensor", s_v, v_mw2)
    mm_group(ps_t, [lambda e: e.matmul(
        ps_t.t[:, 0:1], ones1[:], cols[0:1, 869:870], start=True, stop=True)])
    evac(ps_t, lambda e: e.activation(out=cols[:, 870:871], in_=ps_t.t[:, 0:1],
                                      func=ACT.Identity))
    W("vector", s_ev, s_ev.n)
    R("vector", lambda e: e.reciprocal(out=cols[:, 871:872],
                                       in_=cols[:, 870:871]), s_v, 1)
    w2_hist = [0] * 16
    w2_use = [0] * 16
    W("sync", s_v, s_v.n)
    R("sync", lambda e: e.dma_start(out=w2half(0), in_=w2src(0)), s_wA, 16)
    w2_hist[0] = s_wA.n
    for hf in range(16):
        i, half = hf // 2, hf % 2
        if hf + 1 < 16:
            if hf >= 1:
                W("sync", s_v, w2_use[hf - 1])
            psem = s_wA if (hf + 1) % 2 == 0 else s_wB
            R("sync", lambda e, hf=hf: e.dma_start(
                out=w2half(hf + 1), in_=w2src(hf + 1)), psem, 16)
            w2_hist[hf + 1] = psem.n
        W("vector", s_wA if hf % 2 == 0 else s_wB, w2_hist[hf])
        dq = djq2 if half == 0 else dj2q2
        emit_round(lambda dq=dq: dq[:], lambda hf=hf: w2half(hf),
                   cols[:, 871:872],
                   lambda: w2s1[:], lambda: w2s2[:])
        w2_use[hf] = s_v.n
        W("vector", s_v)
        R("vector", lambda e, dq=dq: e.tensor_scalar(
            out=dq[:], in0=dq[:],
            scalar1=-1.0, scalar2=1.0, op0=ALU.max, op1=ALU.min), s_v, 1)
        v_wq2 = s_v.n
        W("tensor", s_v, v_wq2)
        for kk0 in range(0, 16, 4):
            mm_group(ps_t, [
                (lambda e, dq=dq, kk=kk0 + i2, i2=i2: e.transpose(
                    ps_t.t[:, i2 * 128:(i2 + 1) * 128],
                    dq[:, kk * 128:(kk + 1) * 128],
                    ident[:]))
                for i2 in range(4)])
            W("vector", s_pv)
            for i2 in range(4):
                R("vector", lambda e, kk=half * 16 + kk0 + i2, i2=i2, i=i:
                  e.tensor_copy(
                    out=w2qT[:, kk * 1024 + i * 128:kk * 1024 + (i + 1) * 128],
                    in_=ps_t.t[:, i2 * 128:(i2 + 1) * 128]), s_ev, 1)
            ps_t.mark = s_ev.n
    ev_w2q = s_ev.n

    W("sync", s_sp, s_sp.n)
    HS2 = [hs2, hs2b]
    w2mm_hist = [0] * 16
    hs2_hist = [0] * 16
    v_use_hist = [0] * 16
    W("sync", s_v, s_v.n)
    W("sync", s_ev, s_ev.n)
    R("sync", lambda e: e.dma_start(
        out=HS2[0][:], in_=dhs[0:128, :]), s_wA, 16)
    hs2_hist[0] = s_wA.n
    for t in range(16):
        if t + 1 < 16:
            if t >= 1:
                W("sync", s_v, v_use_hist[t - 1])
            psem = s_wA if (t + 1) % 2 == 0 else s_wB
            R("sync", lambda e, t=t: e.dma_start(
                out=HS2[(t + 1) % 2][:],
                in_=dhs[(t + 1) * 128:(t + 2) * 128, :]), psem, 16)
            hs2_hist[t + 1] = psem.n
        hs2t = HS2[t % 2]
        W("vector", s_wA if t % 2 == 0 else s_wB, hs2_hist[t])
        W("vector", s_out, s_out.n)
        R("vector", lambda e, hs2t=hs2t: e.tensor_reduce(
            out=cols[:, 296:297], in_=hs2t[:], axis=AX, op=ALU.max,
            apply_absolute_value=True), s_v, 1)
        W("vector", s_v)
        R("vector", lambda e: e.tensor_scalar(
            out=cols[:, 296:297], in0=cols[:, 296:297],
            scalar1=1e-5, scalar2=None, op0=ALU.max), s_v, 1)
        W("vector", s_v)
        R("vector", lambda e: e.reciprocal(out=cols[:, 297:298],
                                           in_=cols[:, 296:297]), s_v, 1)
        W("vector", s_v)
        R("vector", lambda e: e.tensor_scalar(
            out=cols[:, 298:299], in0=cols[:, 297:298],
            scalar1=127.0, scalar2=None, op0=ALU.mult), s_v, 1)
        R("vector", lambda e: e.tensor_scalar(
            out=cols[:, 299:300], in0=cols[:, 296:297],
            scalar1=cols[:, 870:871], scalar2=1.0 / 127.0,
            op0=ALU.mult, op1=ALU.mult), s_v, 1)
        W("vector", s_v)
        XQ2 = xq2 if t % 2 == 0 else xq2b
        XQ2T = xq2T if t % 2 == 0 else xq2Tb
        if t >= 2:
            W("vector", s_pv, w2mm_hist[t - 2])
        emit_round(lambda XQ2=XQ2: XQ2[:, 0:2048], lambda hs2t=hs2t: hs2t[:, 0:2048],
                   cols[:, 298:299],
                   lambda: work[:, 0:2048], lambda: work[:, 2048:4096])
        emit_round(lambda XQ2=XQ2: XQ2[:, 2048:4096], lambda hs2t=hs2t: hs2t[:, 2048:4096],
                   cols[:, 298:299],
                   lambda: work[:, 0:2048], lambda: work[:, 2048:4096])
        v_q2 = s_v.n
        v_use_hist[t] = s_v.n
        W("tensor", s_v, v_q2)
        for kk0 in range(0, 32, 4):
            mm_group(ps_t, [
                (lambda e, XQ2=XQ2, kk=kk0 + i2, i2=i2: e.transpose(
                    ps_tb[:, i2 * 128:(i2 + 1) * 128],
                    XQ2[:, kk * 128:(kk + 1) * 128], identb[:]))
                for i2 in range(4)])
            evac(ps_t, lambda e, XQ2T=XQ2T, kk0=kk0: e.tensor_copy(
                out=XQ2T[:, kk0 * 128:(kk0 + 4) * 128],
                in_=ps_tb[:, 0:512]), eng="vector")
        ev_q2T = s_ev.n
        W("tensor", s_ev, ev_q2T)
        W("tensor", s_ev, ev_w2q)
        for ci in range(2):
            pb = ps_a if ci % 2 == 0 else ps_b
            mm_group(pb, [
                (lambda e, XQ2T=XQ2T, kk=kk, ci=ci, pb=pb: e.matmul(
                    pb.t[:, 0:512],
                    XQ2T[:, kk * 128:(kk + 1) * 128],
                    w2qT[:, kk * 1024 + ci * 512:kk * 1024 + (ci + 1) * 512],
                    start=(kk == 0), stop=(kk == 31)))
                for kk in range(32)])
            W("scalar", s_v, s_v.n)
            evac(pb, lambda e, ci=ci, pb=pb: e.activation(
                out=work[:, 1024 + ci * 512:1024 + (ci + 1) * 512],
                in_=pb.t[:, 0:512],
                func=ACT.Identity, scale=cols[:, 299:300]))
        w2mm_hist[t] = s_pv.n
        ev_f = s_ev.n
        W("sync", s_v, s_v.n)
        R("sync", lambda e, t=t: e.dma_start(
            out=work[:, 0:D], in_=dxp[1 + t * 128:1 + (t + 1) * 128, :]), s_rs, 16)
        W("vector", s_rs)
        W("vector", s_ev, ev_f)
        R("vector", lambda e: e.tensor_tensor(
            out=work[:, 2048:3072], in0=work[:, 1024:2048], in1=work[:, 0:D],
            op=ALU.add), s_v, 1)
        W("gpsimd", s_v, s_v.n)
        R("gpsimd", lambda e, t=t: e.dma_start(
            out=dout[t * 128:(t + 1) * 128, :], in_=work[:, 2048:3072]), s_out, 16)
    W("gpsimd", s_out, s_out.n)
    return b


def _prep_common(inputs, core):
    x = np.asarray(inputs["L0_attn_x"], np.float32).reshape(T, D)
    g0 = core * TOWN - 4
    lo, hi = max(0, g0), min(T, g0 + TW)
    xs = np.zeros((TW, D), np.float32)
    xs[lo - g0:hi - g0] = x[lo:hi]

    def colpack(v, groups):
        return np.ascontiguousarray(
            np.asarray(v, np.float32).reshape(groups, 128).T)

    misc = np.zeros((128, 16), np.float32)
    misc[:, 0] = 0.0 if core == 0 else 1.0
    misc[:, 1] = 0.0 if core == NCORES - 1 else 1.0
    misc[:, 2] = 1.0
    for m in range(8):
        misc[:, 3 + m] = 1.0 if m < core else 0.0
    cw = np.asarray(inputs["L0_attn_conv_w"], np.float32).reshape(DI, 4)
    cwc = np.ascontiguousarray(
        cw.reshape(16, 128, 4).transpose(1, 0, 2).reshape(128, 64))
    alog = np.asarray(inputs["L0_attn_A_log"], np.float32)
    alogc = np.ascontiguousarray(
        alog.reshape(16, 128, 16).transpose(1, 0, 2).reshape(128, 256))
    return {
        "x": xs,
        "W_in": np.asarray(inputs["L0_attn_W_in"], np.float32),
        "W_dt": np.asarray(inputs["L0_attn_W_dt"], np.float32),
        "W_xp": np.asarray(inputs["L0_attn_W_xp"], np.float32),
        "bin_c": colpack(inputs["L0_attn_b_in"], 32),
        "bdt_c": colpack(inputs["L0_attn_b_dt"], 16),
        "cb_c": colpack(inputs["L0_attn_conv_b"], 16),
        "cw_c": cwc,
        "bxp_c": np.asarray(inputs["L0_attn_b_xp"], np.float32).reshape(32, 1),
        "alog_c": alogc,
        "misc_c": misc,
    }


def _prep_l2(inputs, core, h0all, dsall):
    d = _prep_common(inputs, core)
    fcw = np.asarray(inputs["L0_ffn_conv_w"], np.float32).reshape(DH, 3)
    fcwc = np.ascontiguousarray(
        fcw.reshape(32, 128, 3).transpose(1, 0, 2).reshape(128, 96))
    abb = np.zeros((4, D), np.float32)
    abb[0] = np.asarray(inputs["L0_attn_alpha"], np.float32)
    abb[1] = np.asarray(inputs["L0_attn_beta"], np.float32)
    abb[2] = np.asarray(inputs["L0_attn_b_out"], np.float32)
    d.update({
        "W_out": np.asarray(inputs["L0_attn_W_out"], np.float32),
        "w1": np.asarray(inputs["L0_ffn_w1"], np.float32),
        "w2": np.asarray(inputs["L0_ffn_w2"], np.float32),
        "dskip_c": np.ascontiguousarray(
            np.asarray(inputs["L0_attn_D_skip"], np.float32).reshape(16, 128).T),
        "abb_r": abb,
        "fcb_c": np.ascontiguousarray(
            np.asarray(inputs["L0_ffn_conv_b"], np.float32).reshape(32, 128).T),
        "fcw_c": fcwc,
        "h0all": h0all,
        "dsall": dsall,
    })
    return d


_CACHE = {}


def _get_prog(summary_only):
    key = bool(summary_only)
    if key not in _CACHE:
        _CACHE[key] = build_core_program(summary_only).finalize()
    return _CACHE[key]


def kernel(**inputs):
    nc1 = _get_prog(True)
    in1 = [_prep_common(inputs, k) for k in range(NCORES)]
    res1 = run_bass_kernel_spmd(nc1, in1, core_ids=list(range(NCORES)))
    h0all = np.concatenate([np.asarray(res1.results[m]["h0sum"], np.float32)
                            for m in range(NCORES)], axis=1)
    dsall = np.concatenate([np.asarray(res1.results[m]["dsum"], np.float32)
                            for m in range(NCORES)], axis=1)
    nc2 = _get_prog(False)
    in2 = []
    for k in range(NCORES):
        d2 = _prep_l2(inputs, k, h0all, dsall)
        for nm in ("xconv_sp", "delta_sp", "z_sp", "bc_sp"):
            d2[nm] = np.asarray(res1.results[k][nm], np.float32)
        in2.append(d2)
    res2 = run_bass_kernel_spmd(nc2, in2, core_ids=list(range(NCORES)))
    out = np.concatenate([np.asarray(res2.results[k]["out"], np.float32)
                          for k in range(NCORES)], axis=0)
    return out.reshape(1, T, D)
